# version 01
# baseline (speedup 1.0000x reference)
import numpy as np
import ml_dtypes
from contextlib import ExitStack

import concourse.bass as bass
import concourse.mybir as mybir
from concourse.bass_utils import run_bass_kernel_spmd

F32 = mybir.dt.float32
BF16 = mybir.dt.bfloat16
I32 = mybir.dt.int32
I8 = mybir.dt.int8
AF = mybir.ActivationFunctionType
ALU = mybir.AluOpType
AX = mybir.AxisListType
bfdt = ml_dtypes.bfloat16

NCORES = 8
SEQ = 2048
DM = 1024
NS = 2
NH = 16
NT = SEQ // 128
BIGNEG = -1e30

def bf(x):
    return np.asarray(x, dtype=np.float32).astype(bfdt).astype(np.float32)


def make_consts():
    c = {}
    j = np.arange(SEQ, dtype=np.float32)
    slopes = np.array([2.0 ** (-8.0 * h / NH) for h in range(1, NH + 1)],
                      dtype=np.float32)
    slopes_bf = bf(slopes)
    ramp_m = np.zeros((128, SEQ), dtype=np.float32)
    ramp_m[0, :] = ramp_m[64, :] = bf(j % 256)
    ramp_m[1, :] = ramp_m[65, :] = bf(256.0 * ((j % 512) >= 256))
    c["ramp2"] = ramp_m
    ka = np.zeros((128, 128), dtype=np.float32)
    ka[0, :] = ka[64, :] = 1.0
    ka[1, :] = ka[65, :] = 1.0
    ka[2, :] = ka[66, :] = bf(np.arange(128))
    c["kaug"] = ka
    c["ident"] = np.eye(128, dtype=np.float32)
    m = np.zeros((128, 128), dtype=np.float32)
    m[np.triu_indices(128, 1)] = BIGNEG
    c["mask_tri"] = m
    z = np.zeros((128, 128), dtype=np.float32)
    z[np.tril_indices(128, -1)] = BIGNEG
    c["zmask"] = z
    c["onescol"] = np.ones((128, 8), dtype=np.float32)
    t_idx = np.arange(128)[:, None] // 16
    col = np.arange(128)[None, :] % 8
    c["gmask"] = (t_idx == col).astype(np.float32)
    tie16 = np.linspace(0.0, 1e-6, 16, dtype=np.float32)
    c["tie_row"] = np.tile(tie16, SEQ // 16)[None, :].astype(np.float32)
    c["ones_row"] = np.ones((1, 128), dtype=np.float32)
    bt = np.zeros((128, 260), dtype=np.float32)
    for h in range(NH):
        for kt in range(16):
            bt[:, h * 16 + kt] = slopes_bf[h] * 128.0 * kt
    bt[:, 256] = 1e-5
    bt[:, 257] = 1e-8
    bt[:, 258] = -0.5
    bt[:, 259] = 0.5
    c["biases"] = bt
    c["qpos"] = (np.arange(16)[None, :] * 128.0 +
                 np.arange(128)[:, None]).astype(np.float32)
    eo = np.zeros((64, 8 * 64), dtype=np.float32)
    for o in range(8):
        for esl in range(8):
            for gl in range(8):
                eo[8 * o + esl, o * 64 + esl * 8 + gl] = 1.0
    c["eoct"] = eo
    e16 = np.zeros((64, 4 * 128), dtype=np.float32)
    for rc in range(4):
        for esl in range(16):
            for gl in range(8):
                e16[16 * rc + esl, rc * 128 + esl * 8 + gl] = 1.0
    c["e16"] = e16
    p_ = np.arange(128)[:, None] % 8
    t_ = np.arange(128)[None, :] // 16
    c["gmaskT"] = (t_ == p_).astype(np.float32)
    c["slope_rows"] = np.repeat(slopes_bf[:, None], SEQ, axis=1)
    c["ones_f32"] = np.ones((1, 128), dtype=np.float32)
    c["onescol_f32"] = np.ones((128, 1), dtype=np.float32)
    return c, slopes_bf


def prep_weights(w):
    d = {}
    T = lambda a: np.ascontiguousarray(np.asarray(a, dtype=np.float32).T)
    C = lambda a: np.ascontiguousarray(np.asarray(a, dtype=np.float32))
    d["wqT"] = T(w["L0_attn_Wq"])
    d["wkT"] = T(w["L0_attn_Wk"])
    d["wvT"] = T(w["L0_attn_Wv"])
    d["woT"] = T(w["L0_attn_Wo"])
    d["ctrl"] = C(np.asarray(w["L0_ffn_controller"]).reshape(DM, 64))
    f1 = np.asarray(w["L0_ffn_f1"]).reshape(DM, 64, 64)
    d["f1"] = C(f1.transpose(1, 0, 2))
    d["f2"] = C(np.asarray(w["L0_ffn_f2"]).reshape(64, 64, DM))
    d["fbias"] = C(np.asarray(w["L0_ffn_bias"]).reshape(64, 64).T)
    d["wdq"] = C(w["L1_attn_W_dq"])
    d["wdqT"] = T(w["L1_attn_W_dq"])
    d["wuq"] = C(w["L1_attn_W_uq"])
    d["wdkvT"] = T(w["L1_attn_W_dkv"])
    d["wuk"] = C(w["L1_attn_W_uk"])
    d["wuv"] = C(w["L1_attn_W_uv"])
    d["woT1"] = T(w["L1_attn_W_o"])
    d["n1w"] = C(np.asarray(w["L1_ffn_norm1_w"]).reshape(1, DM))
    d["w1T"] = T(w["L1_ffn_w1"])
    d["asc"] = C(np.broadcast_to(
        np.asarray(w["L1_ffn_act_scale"]).reshape(1, 1), (128, 1)))
    d["abi"] = C(np.broadcast_to(
        np.asarray(w["L1_ffn_act_bias"]).reshape(1, 1), (128, 1)))
    d["n2w"] = C(np.asarray(w["L1_ffn_norm2_w"]).reshape(1, 4096))
    d["w2T"] = T(w["L1_ffn_w2"])
    return d


ENGS = ("pe", "act", "dve", "pool", "sio")
SEMS = ENGS + ("poold", "siod", "siod2", "poold2", "outd")

class Prog:

    def __init__(self, nc):
        self.nc = nc
        self.q = {e: [] for e in ENGS}
        self.cnt = {s: 0 for s in SEMS}
        self.lastwait = {e: {s: 0 for s in SEMS} for e in ENGS}
        self.sems = {}
        self.dbg = []

    def on(self, eng, f, inc=0):
        if inc == 0 and eng in ("dve", "act", "pool"):
            inc = 1
        if inc:
            self.cnt[eng] += inc
            def g(obj, f=f, k=inc, s=eng):
                f(obj).then_inc(self.sems[s], k)
            self.q[eng].append(g)
        else:
            self.q[eng].append(f)
        return self.cnt[eng]

    def wait(self, eng, src, val):
        if val <= 0 or self.lastwait[eng][src] >= val:
            return
        self.lastwait[eng][src] = val
        self.q[eng].append(
            lambda obj, s=src, v=val: obj.wait_ge(self.sems[s], v))

    def barrier(self, engs=ENGS):
        snap = {e: self.cnt[e] for e in engs}
        for e in engs:
            self.wait(e, e, self.cnt[e])
            self.cnt[e] += 1
            self.q[e].append(lambda obj, s=e: obj.nop().then_inc(self.sems[s], 1))
        for e in engs:
            for s in engs:
                if s != e:
                    self.wait(e, s, snap[s] + 1)

    def dep(self, eng):
        self.wait(eng, eng, self.cnt[eng])

    def marknop(self, eng):
        self.wait(eng, eng, self.cnt[eng])
        self.cnt[eng] += 1
        self.q[eng].append(
            lambda obj, s=eng: obj.nop().then_inc(self.sems[s], 1))
        return self.cnt[eng]

    def dma(self, eng, out, in_, sem=None, **kw):
        assert eng in ("sio", "pool")
        s = sem or (eng + "d")
        self.cnt[s] += 16
        def g(obj, out=out, in_=in_, kw=kw, s=s):
            obj.dma_start(out, in_, **kw).then_inc(self.sems[s], 16)
        self.q[eng].append(g)
        return self.cnt[s]

    def probe(self, mk, name, ap, shape, dtype=F32):
        t = self.nc.dram_tensor(f"dbg_{name}", list(shape), dtype,
                                kind="ExternalOutput")
        self.dbg.append(name)
        self.barrier()
        self.dma("pool", t[:, :], ap)
        self.barrier()

    def finalize(self, ctx):
        for e in SEMS:
            self.sems[e] = ctx.enter_context(self.nc.semaphore(f"sem_{e}"))
        blk = ctx.enter_context(self.nc.Block())
        def run(which):
            def body(obj):
                for f in self.q[which]:
                    f(obj)
            return body
        blk.tensor(run("pe"))
        blk.scalar(run("act"))
        blk.vector(run("dve"))
        blk.gpsimd(run("pool"))
        blk.sync(run("sio"))


class Banks:
    def __init__(self, p, tensors):
        self.p = p
        self.t = tensors
        self.state = [dict(r=[], w=None) for _ in tensors]

    def acquire(self, i, weng="pe"):
        st = self.state[i]
        for (e, c) in st["r"]:
            self.p.wait(weng, e, c)
        st["r"] = []
        return self.t[i]

    def wrote(self, i, eng, cnt):
        self.state[i]["w"] = (eng, cnt)

    def reading(self, i, eng):
        w = self.state[i]["w"]
        if w is not None:
            self.p.wait(eng, w[0], w[1])

    def read_done(self, i, eng, cnt):
        self.state[i]["r"].append((eng, cnt))


class MK:
    def __init__(self, blocks=("l0a", "l0f", "l1a", "l1f"), debug=False,
                 l0_stage=4):
        self.blocks = blocks
        self.l0_stage = l0_stage
        self.debug = debug
        nc = bass.Bass("TRN2", target_bir_lowering=False)
        self.nc = nc
        self.p = Prog(nc)
        self.ctx = ExitStack()
        self.last_pass2_pe = 0
        self.last_ln_act = 0
        self._declare_io()

    def _declare_io(self):
        nc = self.nc
        E = self.ctx.enter_context
        self.x_in = nc.dram_tensor("x_in", [NS * SEQ, DM], F32,
                                   kind="ExternalInput")
        self.y_out = nc.dram_tensor("y_out", [NS * SEQ, DM], F32,
                                    kind="ExternalOutput")
        self.w = {}
        for name, shape in [
            ("wqT", [DM, DM]), ("wkT", [DM, DM]), ("wvT", [DM, DM]),
            ("woT", [DM, DM]), ("ctrl", [DM, 64]), ("f1", [64, DM, 64]),
            ("f2", [64, 64, DM]), ("fbias", [64, 64]),
            ("wdq", [128, DM]), ("wdqT", [DM, 128]), ("wuq", [DM, 128]),
            ("wdkvT", [DM, 128]), ("wuk", [DM, 128]), ("wuv", [DM, 128]),
            ("woT1", [DM, DM]), ("n1w", [1, DM]), ("w1T", [DM, 4096]),
            ("asc", [128, 1]), ("abi", [128, 1]), ("n2w", [1, 4096]),
            ("w2T", [4096, DM]),
        ]:
            self.w[name] = nc.dram_tensor(name, shape, F32, kind="ExternalInput")
        self.cst_in = {}
        for name, shape in [
            ("ramp2", [128, SEQ]),
            ("kaug", [128, 128]), ("ident", [128, 128]),
            ("mask_tri", [128, 128]), ("zmask", [128, 128]),
            ("onescol", [128, 8]), ("gmask", [128, 128]),
            ("tie_row", [1, SEQ]), ("ones_row", [1, 128]),
            ("biases", [128, 260]), ("slope_rows", [NH, SEQ]),
            ("qpos", [128, 16]), ("eoct", [64, 512]), ("e16", [64, 512]),
            ("gmaskT", [128, 128]), ("ones_f32", [1, 128]),
            ("onescol_f32", [128, 1]),
        ]:
            self.cst_in[name] = nc.dram_tensor("c_" + name, shape, F32,
                                               kind="ExternalInput")
        self.stage_a = nc.dram_tensor("stage_a", [NS * SEQ, DM], BF16)
        self.stage_b = nc.dram_tensor("stage_b", [NS * SEQ * 4, DM], BF16)
        self.stage_w = nc.dram_tensor("stage_w", [4096 + DM, 4096], BF16)
        self.stage_c = nc.dram_tensor("stage_c", [NS * SEQ, 4096], BF16)
        self.c_bounce = nc.dram_tensor("c_bounce", [32, 128], BF16)

        sb = lambda n, s, d: E(self.nc.sbuf_tensor(n, s, d))
        self.res = sb("res", [128, NS * NT * 1024], BF16)
        self.xT = sb("xT", [128, 8 * SEQ], BF16)
        self.qk = sb("qk", [128, 4 * SEQ], BF16)
        self.ctxT = sb("ctxT", [128, 8 * SEQ], BF16)
        self.f32a = sb("f32a", [128, 1024], F32)
        self.f32b = sb("f32b", [128, 1024], F32)
        self.wbuf = sb("wbuf", [128, 6 * 1024], BF16)
        self.pbuf = sb("pbuf", [128, 2 * 512], BF16)
        self.scr = sb("scr", [128, 2048], BF16)
        self.util = sb("util", [128, 64], BF16)
        self.stat = sb("stat", [128, 256], F32)
        self.crows = sb("crows", [128, SEQ], BF16)
        self.cst = {}
        for name, shape, dt in [
            ("ramp2", [128, SEQ], BF16),
            ("kaug", [128, 128], BF16), ("ident", [128, 128], BF16),
            ("mask_tri", [128, 128], F32), ("zmask", [128, 128], F32),
            ("onescol", [128, 8], BF16), ("gmask", [128, 128], BF16),
            ("tie_row", [1, SEQ], BF16), ("ones_row", [1, 128], BF16),
            ("biases", [128, 260], F32), ("qpos", [128, 16], F32),
            ("eoct", [64, 512], BF16), ("e16", [64, 512], BF16),
            ("gmaskT", [128, 128], BF16), ("ones_f32", [1, 128], F32),
            ("onescol_f32", [128, 1], F32),
        ]:
            self.cst[name] = sb("k_" + name, shape, dt)
        self.slp = sb("slp", [128, 128], BF16)
        self.i32s = sb("i32s", [64, 256], I32)
        self.fb = sb("fb", [64, 64], F32)
        self.i8t = sb("i8t", [128, 2048], I8)
        self.pb = Banks(self.p, [E(self.nc.psum_tensor(f"pb{i}", [128, 512], F32))
                                 for i in range(6)])
        self.ptr_bf = E(self.nc.psum_tensor("ptrb", [128, 1024], BF16))
        self.ptrb_state = None

    def load_consts(self):
        p = self.p
        for name in self.cst:
            p.dma("pool", self.cst[name][:, :], self.cst_in[name][:, :])

    def load_input(self):
        p = self.p
        for s in range(NS):
            for t in range(NT):
                r0 = s * SEQ + t * 128
                p.dma("pool", self.res_tile(s, t), self.x_in[r0:r0 + 128, :])
        ld = p.cnt["poold"]
        p.wait("sio", "poold", ld)
        for s in range(NS):
            for t in range(NT):
                r0 = s * SEQ + t * 128
                p.dma("sio", self.stage_a[r0:r0 + 128, :], self.res_tile(s, t))

    def res_tile(self, s, t):
        i = s * NT + t
        return self.res[:, i * 1024:(i + 1) * 1024]

    def build_xT(self, s):
        p = self.p
        p.wait("sio", "pe", p.cnt["pe"])
        p.wait("sio", "siod", p.cnt["siod"])
        for dc in range(8):
            p.dma("sio",
                  self.xT[:, dc * SEQ:(dc + 1) * SEQ],
                  self.stage_a[s * SEQ:(s + 1) * SEQ, dc * 128:(dc + 1) * 128],
                  transpose=True)
        return p.cnt["siod"]

    def _ln_from_f32a(self, s, t, nsum, write_fp32_out=False, stage=True,
                      eps=1e-5, nfeat=DM, src=None, stat_base=8):
        p = self.p
        xa = src if src is not None else self.f32a[:, 0:nfeat]
        p.dep("dve")
        p.on("dve", lambda v: v.tensor_reduce(
            out=self.stat[:, 24:25], in_=self.stat[:, stat_base:stat_base + nsum],
            axis=AX.X, op=ALU.add))
        p.dep("dve")
        dv = p.on("dve", lambda v: v.tensor_scalar_mul(
            self.stat[:, 24:25], self.stat[:, 24:25], 1.0 / nfeat))
        p.wait("act", "dve", dv)
        p.dep("act")
        p.on("act", lambda a: a.activation(
            out=self.scr[:, 0:nfeat], in_=xa, func=AF.Square,
            bias=self.stat[:, 24:25], scale=-1.0,
            accum_out=self.stat[:, 25:26]))
        p.dep("act")
        av2 = p.on("act", lambda a: a.activation(
            out=self.stat[:, 26:27], in_=self.stat[:, 25:26], func=AF.Sqrt,
            bias=self.cst["biases"][:, 256:257], scale=1.0 / nfeat), inc=1)
        self.last_ln_act = av2
        p.wait("dve", "act", av2)
        p.on("dve", lambda v: v.reciprocal(self.stat[:, 27:28],
                                           self.stat[:, 26:27]))
        dst = self.f32b[:, 0:nfeat] if write_fp32_out else self.res_tile(s, t)
        p.wait("dve", "outd", getattr(self, "last_out_dma", 0))
        p.on("dve", lambda v: v.tensor_scalar_sub(
            self.f32b[:, 0:nfeat], xa, self.stat[:, 24:25]))
        p.dep("dve")
        c = p.on("dve", lambda v, dst=dst: v.tensor_scalar_mul(
            dst, self.f32b[:, 0:nfeat], self.stat[:, 27:28]), inc=1)
        r0 = s * SEQ + t * 128
        if write_fp32_out:
            p.wait("sio", "dve", c)
            self.last_out_dma = p.dma("sio", self.y_out[r0:r0 + 128, :], dst,
                                      sem="outd")
        elif stage:
            p.wait("sio", "dve", c)
            p.dma("sio", self.stage_a[r0:r0 + 128, :], self.res_tile(s, t))
        return c

    def copy_out(self):
        p = self.p
        p.dep("dve")
        for s in range(NS):
            for t in range(NT):
                c = p.on("dve", lambda v, s=s, t=t: v.tensor_copy(
                    self.f32b[:, 0:1024], self.res_tile(s, t)), inc=1)
                p.wait("sio", "dve", c)
                r0 = s * SEQ + t * 128
                dc = p.dma("sio", self.y_out[r0:r0 + 128, :],
                           self.f32b[:, 0:1024])
                p.wait("dve", "siod", dc)

    def l0_attn(self):
        p, pb = self.p, self.pb
        qT = self.qk[:, 0:SEQ]
        kT = self.qk[:, SEQ:2 * SEQ]
        vb = self.qk[:, 2 * SEQ:4 * SEQ]
        import os
        npairs = int(os.environ.get("L0PAIRS", "8"))
        nproj = os.environ.get("L0PROJ", "qkv")
        for s in range(NS):
            xtld = self.build_xT(s)
            if npairs == 0:
                continue
            for pair in range(npairs):
                wq = self.wbuf[:, 0:1024]
                wk = self.wbuf[:, 1024:2048]
                wv = self.wbuf[:, 2048:3072]
                p.wait("pool", "pe", p.cnt["pe"])
                for (dst, srcn) in ((wq, "wqT"), (wk, "wkT"), (wv, "wvT")):
                    ap = bass.AP(self.w[srcn], pair * 128,
                                 [[1024, 128], [131072, 8], [1, 128]])
                    p.dma("pool", dst.rearrange("p (dc f) -> p dc f", dc=8), ap)
                wld = p.cnt["poold"]
                p.wait("act", "pe", self.last_pass2_pe)
                p.wait("pe", "poold", wld)
                p.wait("pe", "siod", xtld)
                for which, wsl, scl in (((0, wq, 0.125), (1, wk, 1.0))
                                        if "qk" in nproj else ()):
                    dstT = qT if which == 0 else kT
                    for tc in range(4):
                        b = (tc + which * 4) % 6
                        pb.acquire(b)
                        for dc in range(8):
                            p.on("pe", lambda t_, b=b, dc=dc, tc=tc, wsl=wsl: t_.matmul(
                                pb.t[b][:, :],
                                wsl[:, dc * 128:(dc + 1) * 128],
                                self.xT[:, dc * SEQ + tc * 512:
                                        dc * SEQ + tc * 512 + 512],
                                start=(dc == 0), stop=(dc == 7)),
                                 inc=(1 if dc == 7 else 0))
                        mm = p.cnt["pe"]
                        pb.wrote(b, "pe", mm)
                        if "noevac" in nproj:
                            continue
                        pb.reading(b, "act")
                        c = p.on("act", lambda a, b=b, tc=tc, dstT=dstT, scl=scl:
                                 a.activation(
                                     out=dstT[:, tc * 512:tc * 512 + 512],
                                     in_=pb.t[b][:, :], func=AF.Copy, scale=scl),
                                 inc=1)
                        pb.read_done(b, "act", c)
                for tt in (range(NT) if "v" in nproj else ()):
                    b = tt % 6
                    pb.acquire(b)
                    for dc in range(8):
                        p.on("pe", lambda t_, b=b, dc=dc, tt=tt: t_.matmul(
                            pb.t[b][:, 0:128],
                            self.xT[:, dc * SEQ + tt * 128:
                                    dc * SEQ + tt * 128 + 128],
                            wv[:, dc * 128:(dc + 1) * 128],
                            start=(dc == 0), stop=(dc == 7)),
                             inc=(1 if dc == 7 else 0))
                    mm = p.cnt["pe"]
                    pb.wrote(b, "pe", mm)
                    pb.reading(b, "act")
                    c = p.on("act", lambda a, b=b, tt=tt: a.activation(
                        out=vb[:, tt * 128:(tt + 1) * 128],
                        in_=pb.t[b][:, 0:128], func=AF.Copy), inc=1)
                    pb.read_done(b, "act", c)
                if self.l0_stage >= 2:
                    for h2 in range(2):
                        self._l0_head(s, pair * 2 + h2, h2, qT, kT, vb)
            if self.l0_stage >= 4:
                self._l0_wo(s)

    def _l0_head(self, s, h, h2, qT, kT, vb):
        p, pb = self.p, self.pb
        r0 = 64 * h2
        r1 = 64 - r0
        qh = lambda c0, w: qT[r0:r0 + 64, c0:c0 + w]
        kh = lambda c0, w: kT[r0:r0 + 64, c0:c0 + w]
        ucols = self.stat[:, 16:32]
        negu = self.stat[:, 32:96]
        wpart = self.stat[:, 96:100]
        wsum = self.stat[:, 100:101]
        lnw = self.stat[:, 101:102]
        carr = self.stat[:, 104:120]
        chif = self.stat[:, 120:136]
        clo = self.stat[:, 136:152]
        slope = float(bf(2.0 ** (-8.0 * (h + 1) / NH)))
        q2 = self.scr[r0:r0 + 64, :]
        k2 = self.scr[r1:r1 + 64, :]
        cin = self.util[:, 32:64]
        km2 = self.util[:, 0:1]

        p.wait("dve", "pe", self.last_pass2_pe)
        p.on("dve", lambda v: v.memset(self.slp[r1:r1 + 2, 0:128], slope))
        p.dep("act")
        p.on("act", lambda a: a.activation(out=q2, in_=qh(0, SEQ), func=AF.Square))
        a_sq = p.on("act", lambda a: a.activation(out=k2, in_=kh(0, SEQ),
                                                  func=AF.Square), inc=1)
        p.wait("pe", "act", a_sq)
        for half in range(2):
            p.wait("act", "dve", p.cnt["dve"])
            for c in range(2):
                cc = half * 2 + c
                pb.acquire(5)
                mm = p.on("pe", lambda t_, cc=cc: t_.matmul(
                    pb.t[5][0:1, 0:512],
                    self.cst["onescol"][r1:r1 + 64, 0:1],
                    k2[:, cc * 512:(cc + 1) * 512], start=True, stop=True,
                    tile_position=(r1, 0)), inc=1)
                pb.wrote(5, "pe", mm)
                pb.reading(5, "act")
                ca = p.on("act", lambda a, c=c: a.activation(
                    out=self.f32b[0:1, c * 512:(c + 1) * 512],
                    in_=pb.t[5][0:1, 0:512], func=AF.Copy), inc=1)
                pb.read_done(5, "act", ca)
            p.wait("dve", "act", p.cnt["act"])
            p.dep("dve")
            p.on("dve", lambda v, half=half: v.reduce_max(
                self.stat[0:1, half:half + 1], self.f32b[0:1, 0:1024],
                axis=AX.X), inc=1)
        p.dep("dve")
        p.on("dve", lambda v: v.tensor_max(
            self.stat[0:1, 0:1], self.stat[0:1, 0:1], self.stat[0:1, 1:2]))
        p.dep("dve")
        dv = p.on("dve", lambda v: v.tensor_copy(self.util[0:1, 4:5],
                                                 self.stat[0:1, 0:1]), inc=1)
        pb.acquire(5)
        p.wait("pe", "dve", dv)
        mm = p.on("pe", lambda t_: t_.matmul(
            pb.t[5][:, 0:1], self.cst["ones_row"][0:1, :],
            self.util[0:1, 4:5], start=True, stop=True), inc=1)
        pb.wrote(5, "pe", mm)
        pb.reading(5, "act")
        ca = p.on("act", lambda a: a.activation(out=km2, in_=pb.t[5][:, 0:1],
                                                func=AF.Copy), inc=1)
        pb.read_done(5, "act", ca)
        pb.acquire(5)
        p.wait("pe", "act", ca)
        for qt in range(NT):
            p.on("pe", lambda t_, qt=qt: t_.matmul(
                pb.t[5][:, qt:qt + 1], q2[:, qt * 128:(qt + 1) * 128],
                km2[r0:r0 + 64, 0:1], start=True, stop=True,
                tile_position=(r0, 0)), inc=(1 if qt == NT - 1 else 0))
        mm = p.cnt["pe"]
        pb.wrote(5, "pe", mm)
        pb.reading(5, "act")
        ca = p.on("act", lambda a: a.activation(out=ucols, in_=pb.t[5][:, 0:16],
                                                func=AF.Sqrt), inc=1)
        pb.read_done(5, "act", ca)
        p.wait("dve", "act", ca)
        p.dep("dve")
        p.on("dve", lambda v: v.scalar_tensor_tensor(
            out=ucols, in0=self.cst["qpos"][:, :], scalar=slope, in1=ucols,
            op0=ALU.mult, op1=ALU.add))
        p.dep("dve")
        for c in range(4):
            p.on("dve", lambda v, c=c: v.tensor_scalar_mul(
                negu[:, c * 16:(c + 1) * 16], ucols, -1.0))
            p.dep("dve")
            p.on("dve", lambda v, c=c: v.tensor_scalar_add(
                negu[:, c * 16:(c + 1) * 16], negu[:, c * 16:(c + 1) * 16],
                float(slope * 512.0 * c - 2.0)))
        dv_negu = p.marknop("dve")

        p.wait("act", "dve", dv_negu)
        for qt in range(NT):
            kend = (qt + 1) * 128
            nk = (kend + 511) // 512
            for c in range(nk):
                Lc = min(512, kend - c * 512)
                b = c % 2
                pb.acquire(b)
                p.on("pe", lambda t_, b=b, qt=qt, c=c, Lc=Lc: t_.matmul(
                    pb.t[b][:, 0:Lc], qh(qt * 128, 128),
                    kh(c * 512, Lc), start=True, stop=False,
                    tile_position=(r0, 0)))
                mm = p.on("pe", lambda t_, b=b, c=c, Lc=Lc: t_.matmul(
                    pb.t[b][:, 0:Lc],
                    self.slp[r1:r1 + 2, 0:128],
                    self.cst["ramp2"][r1:r1 + 2, c * 512:c * 512 + Lc],
                    start=False, stop=True, tile_position=(r1, 0)), inc=1)
                pb.wrote(b, "pe", mm)
                if c == nk - 1:
                    pb.reading(b, "dve")
                    dmm = p.on("dve", lambda v, b=b, Lc=Lc: v.tensor_add(
                        pb.t[b][:, Lc - 128:Lc], pb.t[b][:, Lc - 128:Lc],
                        self.cst["mask_tri"][:, :]), inc=1)
                    pb.read_done(b, "dve", dmm)
                    p.wait("act", "dve", dmm)
                pb.reading(b, "act")
                p.dep("act")
                ca = p.on("act", lambda a, b=b, c=c, qt=qt, Lc=Lc: a.activation(
                    out=self.pbuf[:, 0:Lc], in_=pb.t[b][:, 0:Lc], func=AF.Exp,
                    bias=negu[:, c * 16 + qt:c * 16 + qt + 1],
                    accum_out=wpart[:, c:c + 1]), inc=1)
                pb.read_done(b, "act", ca)
            p.wait("dve", "act", p.cnt["act"])
            p.dep("dve")
            dv = p.on("dve", lambda v, nk=nk: v.tensor_reduce(
                out=wsum, in_=wpart[:, 0:nk], axis=AX.X, op=ALU.add))
            p.wait("act", "dve", dv)
            ca = p.on("act", lambda a: a.activation(out=lnw, in_=wsum,
                                                    func=AF.Ln), inc=1)
            p.wait("dve", "act", ca)
            p.dep("dve")
            p.on("dve", lambda v, qt=qt: v.tensor_sub(
                carr[:, qt:qt + 1], negu[:, qt:qt + 1], lnw))
        cin_even = bass.AP(self.util, 32, [[64, 128], [2, 16]])
        cin_odd = bass.AP(self.util, 33, [[64, 128], [2, 16]])
        p.dep("dve")
        p.on("dve", lambda v: v.tensor_copy(cin_even, carr))
        p.dep("dve")
        p.on("dve", lambda v: v.tensor_copy(chif, cin_even))
        p.dep("dve")
        p.on("dve", lambda v: v.tensor_sub(clo, carr, chif))
        p.dep("dve")
        dv = p.on("dve", lambda v: v.tensor_copy(cin_odd, clo), inc=1)
        if self.ptrb_state is not None:
            self.p.wait("pe", self.ptrb_state[0], self.ptrb_state[1])
        p.wait("pe", "dve", dv)
        mm = p.on("pe", lambda t_: t_.transpose(
            self.ptr_bf[0:32, 0:128], cin, self.cst["ident"][:, :]), inc=1)
        p.wait("dve", "pe", mm)
        dv = p.on("dve", lambda v: v.tensor_copy(self.scr[0:32, 0:128],
                                                 self.ptr_bf[0:32, 0:128]),
                  inc=1)
        self.ptrb_state = ("dve", dv)
        p.wait("pool", "pe", self.last_pass2_pe)
        p.wait("pool", "dve", dv)
        p.dma("pool", self.c_bounce[0:32, 0:128], self.scr[0:32, 0:128])
        p.wait("pool", "poold", p.cnt["poold"])
        src = bass.AP(self.c_bounce, 0, [[128, 2], [256, 16], [1, 128]])
        p.dma("pool", self.crows[r1:r1 + 2, 0:SEQ], src)
        crows_done = p.dma("pool", self.crows[r1 + 2:r1 + 3, 0:SEQ],
                           self.cst_in["slope_rows"][h:h + 1, :])

        if self.l0_stage < 3:
            return
        p.wait("pe", "poold", crows_done)
        cb = 4
        av_last = {0: 0, 1: 0}
        for qs in range(4):
            pb.acquire(cb)
            nkt = 4 * (qs + 1)
            for kt in range(nkt):
                a0 = max(0, 128 * (kt - 4 * qs))
                b = 2 + kt % 2
                pb.acquire(b)
                p.on("pe", lambda t_, b=b, kt=kt, qs=qs, a0=a0: t_.matmul(
                    pb.t[b][:, a0:512], kh(kt * 128, 128),
                    qh(qs * 512 + a0, 512 - a0), start=True, stop=False,
                    tile_position=(r0, 0)))
                mm = p.on("pe", lambda t_, b=b, qs=qs, a0=a0: t_.matmul(
                    pb.t[b][:, a0:512], self.cst["kaug"][r1:r1 + 3, 0:128],
                    self.crows[r1:r1 + 3, qs * 512 + a0: (qs + 1) * 512],
                    start=False, stop=True, tile_position=(r1, 0)), inc=1)
                pb.wrote(b, "pe", mm)
                half = kt % 2
                ph = self.pbuf[:, half * 512:half * 512 + 512]
                if kt >= 4 * qs:
                    pb.reading(b, "dve")
                    dz = p.on("dve", lambda v, b=b, a0=a0: v.tensor_add(
                        pb.t[b][:, a0:a0 + 128], pb.t[b][:, a0:a0 + 128],
                        self.cst["zmask"][:, :]), inc=1)
                    pb.read_done(b, "dve", dz)
                    p.wait("act", "dve", dz)
                p.wait("act", "pe", av_last[half])
                pb.reading(b, "act")
                ca = p.on("act", lambda a, b=b, kt=kt, a0=a0, ph=ph, h=h: a.activation(
                    out=ph[:, a0:512], in_=pb.t[b][:, a0:512], func=AF.Exp,
                    bias=self.cst["biases"][:, h * 16 + kt:h * 16 + kt + 1]),
                    inc=1)
                pb.read_done(b, "act", ca)
                use_eng, use_cnt = "act", ca
                p.wait("pe", use_eng, use_cnt)
                av = p.on("pe", lambda t_, kt=kt, a0=a0, ph=ph, qs=qs,
                          nkt=nkt: t_.matmul(
                    pb.t[cb][0:64, a0:512],
                    vb[:, kt * 128 + 64 * h2: kt * 128 + 64 * h2 + 64],
                    ph[:, a0:512], start=(kt == 0), stop=(kt == nkt - 1)),
                    inc=1)
                av_last[half] = av
            pb.wrote(cb, "pe", p.cnt["pe"])
            pb.reading(cb, "act")
            pr = h // 2
            ca = p.on("act", lambda a, pr=pr, qs=qs: a.activation(
                out=self.ctxT[64 * h2:64 * h2 + 64,
                              pr * SEQ + qs * 512: pr * SEQ + qs * 512 + 512],
                in_=pb.t[cb][0:64, :], func=AF.Copy), inc=1)
            pb.read_done(cb, "act", ca)
        self.last_pass2_pe = p.cnt["pe"]

    def _l0_wo(self, s):
        p, pb = self.p, self.pb
        p.wait("pool", "pe", p.cnt["pe"])
        ap = bass.AP(self.w["woT"], 0, [[1024, 128], [131072, 8], [1, 1024]])
        wld = p.dma("pool",
                    self.xT[:, 0:8192].rearrange("p (dc f) -> p dc f", dc=8),
                    ap)
        p.wait("pe", "poold", wld)
        p.wait("pe", "act", p.cnt["act"])
        for qt in range(NT):
            for ech in range(2):
                b = (2 * qt + ech) % 2
                pb.acquire(b)
                for dc in range(8):
                    p.on("pe", lambda t_, b=b, dc=dc, qt=qt, ech=ech: t_.matmul(
                        pb.t[b][:, :],
                        self.ctxT[:, dc * SEQ + qt * 128: dc * SEQ + qt * 128 + 128],
                        self.xT[:, dc * 1024 + ech * 512: dc * 1024 + ech * 512 + 512],
                        start=(dc == 0), stop=(dc == 7)),
                         inc=(1 if dc == 7 else 0))
                mm = p.cnt["pe"]
                pb.wrote(b, "pe", mm)
                pb.reading(b, "dve")
                p.wait("dve", "act", self.last_ln_act)
                c = p.on("dve", lambda v, b=b, qt=qt, ech=ech: v.scalar_tensor_tensor(
                    out=self.f32a[:, ech * 512:(ech + 1) * 512],
                    in0=pb.t[b][:, :], scalar=0.0,
                    in1=self.res_tile(s, qt)[:, ech * 512:(ech + 1) * 512],
                    op0=ALU.add, op1=ALU.add,
                    accum_out=self.stat[:, 8 + ech:9 + ech]), inc=1)
                pb.read_done(b, "dve", c)
            self._ln_from_f32a(s, qt, nsum=2, write_fp32_out=False, stage=True)

    def l0_ffn(self):
        p, pb = self.p, self.pb
        permT = self.ctxT[0:64, 12288:16384]
        SEL = self.xT
        cbuf = self.wbuf[:, 4096:4608]
        ap = bass.AP(self.w["ctrl"], 0, [[64, 128], [8192, 8], [1, 64]])
        p.dma("pool", cbuf.rearrange("p (dc f) -> p dc f", dc=8), ap)
        wld = p.dma("pool", self.fb[:, :], self.w["fbias"][:, :])
        p.wait("pe", "poold", wld)
        xs = [self.wbuf[:, i * 512:(i + 1) * 512] for i in range(8)]
        ld_cnt = 0
        for tc in range(8):
            pb.acquire(5)
            p.wait("sio", "pe", ld_cnt)
            for dc in range(8):
                ld = p.dma("sio", xs[dc],
                           self.stage_a[tc * 512:(tc + 1) * 512,
                                        dc * 128:(dc + 1) * 128],
                           transpose=True)
            p.wait("pe", "siod", p.cnt["siod"])
            for dc in range(8):
                mm = p.on("pe", lambda t_, dc=dc: t_.matmul(
                    pb.t[5][0:64, 0:512], cbuf[:, dc * 64:(dc + 1) * 64],
                    xs[dc][:, :], start=(dc == 0), stop=False),
                    inc=1)
            ld_cnt = p.cnt["pe"]
            mm = p.on("pe", lambda t_, tc=tc: t_.matmul(
                pb.t[5][0:64, 0:512], self.cst["ones_row"][0:1, 0:64],
                self.cst["tie_row"][0:1, (tc % 4) * 512:(tc % 4) * 512 + 512],
                start=False, stop=True), inc=1)
            pb.wrote(5, "pe", mm)
            gmax = self.stat[0:64, 200:232]
            pb.reading(5, "dve")
            p.dep("dve")
            p.on("dve", lambda v: v.reduce_max(
                gmax, pb.t[5][0:64, 0:512].rearrange("p (g t) -> p g t", t=16),
                axis=AX.X), inc=1)
            for tl in range(16):
                p.dep("dve")
                p.on("dve", lambda v, tl=tl: v.tensor_sub(
                    self.f32a[0:64, 0:32],
                    bass.AP(pb.t[5], tl, [[512, 64], [16, 32]]),
                    gmax), inc=1)
                dvs = p.marknop("dve")
                p.wait("act", "dve", dvs)
                p.dep("act")
                aa = p.on("act", lambda a: a.activation(
                    out=self.f32a[0:64, 0:32], in_=self.f32a[0:64, 0:32],
                    func=AF.Sign), inc=1)
                p.wait("dve", "act", aa)
                p.dep("dve")
                c = p.on("dve", lambda v, tc=tc, tl=tl: v.tensor_scalar_add(
                    bass.AP(self.ctxT, 12288 + tc * 512 + tl,
                            [[16384, 64], [16, 32]]),
                    self.f32a[0:64, 0:32], 1.0), inc=1)
            pb.read_done(5, "dve", p.cnt["dve"])
        pob_last = {0: 0, 1: 0}
        innT_last = {0: 0, 1: 0}
        half_dma = {0: 0, 1: 0}
        for o in range(8):
            p.barrier(("pe", "act", "dve", "pool"))
            ap1 = bass.AP(self.w["f1"], o * 8 * DM * 64,
                          [[64, 128], [65536, 8], [8192, 8], [1, 64]])
            p.dma("pool", self.ctxT[:, 0:4096].rearrange(
                "p (es dc f) -> p es dc f", es=8, dc=8), ap1)
            ap2 = bass.AP(self.w["f2"], o * 8 * 64 * DM,
                          [[1024, 64], [65536, 8], [1, 1024]])
            wld = p.dma("pool", self.ctxT[0:64, 4096:12288].rearrange(
                "p (es d) -> p es d", es=8), ap2)
            p.wait("pe", "poold", wld)
            for xt in range(32):
                s_, t_ = divmod(xt, NT)
                eb = 2 + xt % 2
                pob = self.scr[:, 512 + (xt % 2) * 128:512 + (xt % 2) * 128 + 64]
                pb.acquire(eb)
                mm = p.on("pe", lambda t_e, eb=eb, xt=xt, o=o: t_e.matmul(
                    pb.t[eb][:, 0:64], permT[:, xt * 128:(xt + 1) * 128],
                    self.cst["eoct"][:, o * 64:o * 64 + 64],
                    start=True, stop=True), inc=1)
                pb.wrote(eb, "pe", mm)
                pb.reading(eb, "dve")
                p.wait("dve", "pe", pob_last[xt % 2])
                p.dep("dve")
                dv = p.on("dve", lambda v, eb=eb, pob=pob: v.tensor_mul(
                    pob, pb.t[eb][:, 0:64], self.cst["gmask"][:, 0:64]),
                    inc=1)
                pb.read_done(eb, "dve", dv)
                sb_ = xt % 2
                pb.acquire(sb_)
                p.wait("pe", "dve", dv)
                for dc in range(8):
                    p.on("pe", lambda t_e, sb_=sb_, dc=dc, s_=s_, t_=t_,
                         pob=pob: t_e.matmul(
                        pb.t[sb_][:, dc * 64:(dc + 1) * 64],
                        self.res_tile(s_, t_)[:, dc * 128:(dc + 1) * 128],
                        pob, start=True, stop=True),
                        inc=(1 if dc == 7 else 0))
                mm = p.cnt["pe"]
                pob_last[xt % 2] = mm
                pb.wrote(sb_, "pe", mm)
                eng = "dve" if xt % 2 == 0 else "act"
                pb.reading(sb_, eng)
                p.dep(eng)
                dst = bass.AP(self.xT, 8 * xt, [[16384, 128], [2048, 8],
                                                [256, 8], [1, 8]])
                if eng == "dve":
                    c = p.on("dve", lambda v, sb_=sb_, dst=dst: v.tensor_copy(
                        dst, pb.t[sb_][:, 0:512].rearrange(
                            "p (dc es g) -> p dc es g", dc=8, es=8)), inc=1)
                else:
                    c = p.on("act", lambda a, sb_=sb_, dst=dst: a.activation(
                        out=dst, in_=pb.t[sb_][:, 0:512].rearrange(
                            "p (dc es g) -> p dc es g", dc=8, es=8),
                        func=AF.Copy), inc=1)
                pb.read_done(sb_, eng, c)
            p.wait("pe", "dve", p.cnt["dve"])
            p.wait("pe", "act", p.cnt["act"])
            for es_l in range(8):
                es = 8 * o + es_l
                pb.acquire(4)
                for dc in range(8):
                    p.on("pe", lambda t_e, es_l=es_l, dc=dc: t_e.matmul(
                        pb.t[4][0:64, 0:256],
                        self.ctxT[:, es_l * 512 + dc * 64:
                                  es_l * 512 + dc * 64 + 64],
                        SEL[:, dc * 2048 + es_l * 256:
                            dc * 2048 + es_l * 256 + 256],
                        start=(dc == 0), stop=(dc == 7)),
                        inc=(1 if dc == 7 else 0))
                mm = p.cnt["pe"]
                pb.wrote(4, "pe", mm)
                innT = self.scr[0:64, (es_l % 2) * 256:(es_l % 2) * 256 + 256]
                pb.reading(4, "act")
                p.wait("act", "pe", innT_last[es_l % 2])
                p.dep("act")
                ca = p.on("act", lambda a, innT=innT, es=es: a.activation(
                    out=innT, in_=pb.t[4][0:64, 0:256], func=AF.Relu,
                    bias=self.fb[:, es:es + 1]), inc=1)
                pb.read_done(4, "act", ca)
                p.wait("pe", "act", ca)
                for gch in range(2):
                    for dch in range(2):
                        fb_ = gch
                        pb.acquire(fb_)
                        mm = p.on("pe", lambda t_e, fb_=fb_, innT=innT,
                                  es_l=es_l, gch=gch, dch=dch: t_e.matmul(
                            pb.t[fb_][:, 0:512],
                            innT[:, gch * 128:gch * 128 + 128],
                            self.ctxT[0:64, 4096 + es_l * 1024 + dch * 512:
                                      4096 + es_l * 1024 + dch * 512 + 512],
                            start=True, stop=True), inc=1)
                        pb.wrote(fb_, "pe", mm)
                        innT_last[es_l % 2] = mm
                        st = self.pbuf[:, (dch % 2) * 512:(dch % 2) * 512 + 512]
                        eng = "dve" if dch == 0 else "act"
                        pb.reading(fb_, eng)
                        p.wait(eng, "siod", half_dma[dch])
                        p.dep(eng)
                        if eng == "dve":
                            c = p.on("dve", lambda v, fb_=fb_, st=st:
                                     v.tensor_copy(st, pb.t[fb_][:, 0:512]),
                                     inc=1)
                        else:
                            c = p.on("act", lambda a, fb_=fb_, st=st:
                                     a.activation(out=st,
                                                  in_=pb.t[fb_][:, 0:512],
                                                  func=AF.Copy), inc=1)
                        pb.read_done(fb_, eng, c)
                        p.wait("sio", eng, c)
                        p.wait("sio", "siod", p.cnt["siod"])
                        r0 = (es * 256 + gch * 128)
                        half_dma[dch] = p.dma(
                            "sio", self.stage_b[r0:r0 + 128,
                                                dch * 512:(dch + 1) * 512],
                            st)
        p.barrier(("pe", "act", "dve", "pool"))
        stb_done = p.cnt["siod"]
        iv = [self.wbuf[:, i * 512:(i + 1) * 512] for i in range(4)]
        iv_cnt = [0, 0, 0, 0]
        scat_pe = 0
        for xt in range(32):
            s_, t_ = divmod(xt, NT)
            p.wait("dve", "pe", scat_pe)
            for rc in range(4):
                eb = 2 + rc % 2
                pb.acquire(eb)
                mm = p.on("pe", lambda t_e, eb=eb, rc=rc, xt=xt: t_e.matmul(
                    pb.t[eb][:, 0:128],
                    self.cst["e16"][:, rc * 128:(rc + 1) * 128],
                    permT[:, xt * 128:(xt + 1) * 128],
                    start=True, stop=True), inc=1)
                pb.wrote(eb, "pe", mm)
                pb.reading(eb, "dve")
                p.dep("dve")
                dv = p.on("dve", lambda v, eb=eb, rc=rc: v.tensor_mul(
                    self.scr[:, 1024 + rc * 128:1024 + rc * 128 + 128],
                    pb.t[eb][:, 0:128], self.cst["gmaskT"][:, :]), inc=1)
                pb.read_done(eb, "dve", dv)
            pt_ready = p.cnt["dve"]
            p.wait("sio", "siod", stb_done)
            for dch in range(2):
                ab = dch
                pb.acquire(ab)
                p.wait("pe", "dve", pt_ready)
                p.wait("sio", "pe", max(iv_cnt))
                p.wait("sio", "siod", p.cnt["siod"])
                for rc in range(4):
                    r0 = 16 * rc * 256 + 8 * xt
                    src = bass.AP(self.stage_b, r0 * DM + dch * 512,
                                  [[256 * DM, 16], [DM, 8], [1, 512]])
                    p.dma("sio", iv[rc], src)
                p.wait("pe", "siod", p.cnt["siod"])
                for rc in range(4):
                    mm = p.on("pe", lambda t_e, ab=ab, rc=rc: t_e.matmul(
                        pb.t[ab][:, 0:512],
                        self.scr[:, 1024 + rc * 128:1024 + rc * 128 + 128],
                        iv[rc][:, :], start=(rc == 0), stop=(rc == 3)),
                        inc=1)
                    iv_cnt[rc] = mm
                pb.wrote(ab, "pe", p.cnt["pe"])
                scat_pe = p.cnt["pe"]
                pb.reading(ab, "dve")
                p.wait("dve", "act", self.last_ln_act)
                p.dep("dve")
                c = p.on("dve", lambda v, ab=ab, s_=s_, t_=t_, dch=dch:
                         v.scalar_tensor_tensor(
                    out=self.f32a[:, dch * 512:(dch + 1) * 512],
                    in0=pb.t[ab][:, 0:512], scalar=0.0,
                    in1=self.res_tile(s_, t_)[:, dch * 512:(dch + 1) * 512],
                    op0=ALU.add, op1=ALU.add,
                    accum_out=self.stat[:, 8 + dch:9 + dch]), inc=1)
                pb.read_done(ab, "dve", c)
            self._ln_from_f32a(s_, t_, nsum=2, write_fp32_out=False,
                               stage=True)

    def l1_attn(self):
        p, pb = self.p, self.pb
        BT = self.qk[:, 0:SEQ]
        CKT = self.qk[:, SEQ:2 * SEQ]
        CKV = self.qk[:, 2 * SEQ:3 * SEQ]
        QH = self.qk[:, 3 * SEQ:3 * SEQ + 1024]
        keff = self.wbuf[:, 0:1024]
        veff = self.wbuf[:, 1024:2048]
        gbuf = self.wbuf[:, 2048:4096]
        wdqT = self.xT[:, 0:1024]
        wdkvT = self.xT[:, 1024:2048]
        xs = [self.xT[:, 2048 + i * 512:2048 + (i + 1) * 512] for i in range(8)]

        wa = self.xT[:, 8192:9216]
        wb_ = self.xT[:, 9216:10240]
        wc = self.xT[:, 10240:11264]
        p.wait("pool", "pe", p.cnt["pe"])
        for (dst, srcn) in ((wa, "wuq"), (wb_, "wuk")):
            ap = bass.AP(self.w[srcn], 0, [[128, 128], [16384, 8], [1, 128]])
            p.dma("pool", dst.rearrange("p (dc f) -> p dc f", dc=8), ap)
        ld = p.dma("pool", wc, self.w["wdq"][:, :])
        p.wait("pe", "poold", ld)
        Sb = self.xT[:, 11264:11392]
        pb.acquire(5)
        for dc in range(8):
            p.on("pe", lambda t_, dc=dc: t_.matmul(
                pb.t[5][:, 0:128], wa[:, dc * 128:(dc + 1) * 128],
                wb_[:, dc * 128:(dc + 1) * 128],
                start=(dc == 0), stop=(dc == 7)), inc=(1 if dc == 7 else 0))
        pb.wrote(5, "pe", p.cnt["pe"])
        pb.reading(5, "act")
        ca = p.on("act", lambda a: a.activation(out=Sb, in_=pb.t[5][:, 0:128],
                                                func=AF.Copy), inc=1)
        pb.read_done(5, "act", ca)
        p.wait("pe", "act", ca)
        for ic in range(8):
            b = ic % 2
            pb.acquire(b)
            mm = p.on("pe", lambda t_, b=b, ic=ic: t_.matmul(
                pb.t[b][:, 0:128], wc[:, ic * 128:(ic + 1) * 128], Sb,
                start=True, stop=True), inc=1)
            pb.wrote(b, "pe", mm)
            pb.reading(b, "act")
            ca = p.on("act", lambda a, b=b, ic=ic: a.activation(
                out=keff[:, ic * 128:(ic + 1) * 128], in_=pb.t[b][:, 0:128],
                func=AF.Copy), inc=1)
            pb.read_done(b, "act", ca)
        for h in range(NH):
            rr = 64 * (h % 2)
            b = 2 + h % 2
            pb.acquire(b)
            mm = p.on("pe", lambda t_, b=b, h=h, rr=rr: t_.matmul(
                pb.t[b][:, 0:128],
                wa[rr:rr + 64, (h // 2) * 128:(h // 2) * 128 + 128],
                keff[rr:rr + 64, (h // 2) * 128:(h // 2) * 128 + 128],
                start=True, stop=True, tile_position=(rr, 0)), inc=1)
            pb.wrote(b, "pe", mm)
            pb.reading(b, "act")
            ca = p.on("act", lambda a, b=b, h=h: a.activation(
                out=gbuf[:, h * 128:(h + 1) * 128], in_=pb.t[b][:, 0:128],
                func=AF.Copy), inc=1)
            pb.read_done(b, "act", ca)
        p.wait("pool", "pe", p.cnt["pe"])
        p.dma("pool", wa.rearrange("p (dc f) -> p dc f", dc=8),
              bass.AP(self.w["wuv"], 0, [[128, 128], [16384, 8], [1, 128]]))
        ld = p.dma("pool", self.xT[:, 0:8192].rearrange(
            "p (dc f) -> p dc f", dc=8),
            bass.AP(self.w["woT1"], 0, [[1024, 128], [131072, 8], [1, 1024]]))
        p.wait("pe", "poold", ld)
        for ech in range(2):
            b = ech % 2
            pb.acquire(b)
            for dc in range(8):
                p.on("pe", lambda t_, b=b, dc=dc, ech=ech: t_.matmul(
                    pb.t[b][:, 0:512], wa[:, dc * 128:(dc + 1) * 128],
                    self.xT[:, dc * 1024 + ech * 512:
                            dc * 1024 + ech * 512 + 512],
                    start=(dc == 0), stop=(dc == 7)), inc=(1 if dc == 7 else 0))
            pb.wrote(b, "pe", p.cnt["pe"])
            pb.reading(b, "act")
            ca = p.on("act", lambda a, b=b, ech=ech: a.activation(
                out=veff[:, ech * 512:(ech + 1) * 512], in_=pb.t[b][:, 0:512],
                func=AF.Copy), inc=1)
            pb.read_done(b, "act", ca)
        p.wait("pool", "pe", p.cnt["pe"])
        p.dma("pool", wdqT.rearrange("p (dc f) -> p dc f", dc=8),
              bass.AP(self.w["wdqT"], 0, [[128, 128], [16384, 8], [1, 128]]))
        ld = p.dma("pool", wdkvT.rearrange("p (dc f) -> p dc f", dc=8),
                   bass.AP(self.w["wdkvT"], 0, [[128, 128], [16384, 8], [1, 128]]))
        p.wait("pe", "poold", ld)

        for s in range(NS):
            self._l1_seq(s, BT, CKT, CKV, QH, wdqT, wdkvT, xs, gbuf, veff)

    def _l1_seq(self, s, BT, CKT, CKV, QH, wdqT, wdkvT, xs, gbuf, veff):
        p, pb = self.p, self.pb
        km2 = self.util[:, 8:9]
        numax = self.stat[:, 32:33]
        wpart = self.stat[:, 96:100]
        recw = self.stat[:, 128:256]
        ld_gate = p.cnt["pe"]
        p.wait("sio", "poold", p.cnt["poold"])
        for tc in range(4):
            p.wait("sio", "pe", ld_gate)
            for dc in range(8):
                p.dma("sio", xs[dc],
                      self.stage_a[s * SEQ + tc * 512:s * SEQ + (tc + 1) * 512,
                                   dc * 128:(dc + 1) * 128], transpose=True)
            p.wait("pe", "siod", p.cnt["siod"])
            for which, wsl, dstT in ((0, wdqT, BT), (1, wdkvT, CKT)):
                b = which
                pb.acquire(b)
                for dc in range(8):
                    p.on("pe", lambda t_, b=b, dc=dc, wsl=wsl: t_.matmul(
                        pb.t[b][:, 0:512], wsl[:, dc * 128:(dc + 1) * 128],
                        xs[dc][:, :], start=(dc == 0), stop=(dc == 7)),
                        inc=(1 if dc == 7 else 0))
                pb.wrote(b, "pe", p.cnt["pe"])
                pb.reading(b, "act")
                ca = p.on("act", lambda a, b=b, tc=tc, dstT=dstT: a.activation(
                    out=dstT[:, tc * 512:(tc + 1) * 512], in_=pb.t[b][:, 0:512],
                    func=AF.Copy), inc=1)
                pb.read_done(b, "act", ca)
            ld_gate = p.cnt["pe"]
        proj_act = p.cnt["act"]
        p.wait("pe", "act", proj_act)
        for kt in range(NT):
            if self.ptrb_state is not None:
                p.wait("pe", self.ptrb_state[0], self.ptrb_state[1])
            mm = p.on("pe", lambda t_, kt=kt: t_.transpose(
                self.ptr_bf[:, 0:128], CKT[:, kt * 128:(kt + 1) * 128],
                self.cst["ident"][:, :]), inc=1)
            p.wait("act", "pe", mm)
            p.dep("act")
            ca = p.on("act", lambda a, kt=kt: a.activation(
                out=CKV[:, kt * 128:(kt + 1) * 128], in_=self.ptr_bf[:, 0:128],
                func=AF.Copy), inc=1)
            self.ptrb_state = ("act", ca)
        k2 = self.scr[:, 0:SEQ]
        p.dep("act")
        a_sq = p.on("act", lambda a: a.activation(out=k2, in_=CKT[:, 0:SEQ],
                                                  func=AF.Square), inc=1)
        p.wait("pe", "act", a_sq)
        for half in range(2):
            p.wait("act", "dve", p.cnt["dve"])
            for c in range(2):
                cc = half * 2 + c
                pb.acquire(5)
                mm = p.on("pe", lambda t_, cc=cc: t_.matmul(
                    pb.t[5][0:1, 0:512], self.cst["onescol"][:, 0:1],
                    k2[:, cc * 512:(cc + 1) * 512], start=True, stop=True),
                    inc=1)
                pb.wrote(5, "pe", mm)
                pb.reading(5, "act")
                ca = p.on("act", lambda a, c=c: a.activation(
                    out=self.f32b[0:1, c * 512:(c + 1) * 512],
                    in_=pb.t[5][0:1, 0:512], func=AF.Copy), inc=1)
                pb.read_done(5, "act", ca)
            p.wait("dve", "act", p.cnt["act"])
            p.dep("dve")
            p.on("dve", lambda v, half=half: v.reduce_max(
                self.stat[0:1, half:half + 1], self.f32b[0:1, 0:1024],
                axis=AX.X), inc=1)
        p.dep("dve")
        p.on("dve", lambda v: v.tensor_max(
            self.stat[0:1, 0:1], self.stat[0:1, 0:1], self.stat[0:1, 1:2]))
        p.dep("dve")
        dv = p.on("dve", lambda v: v.tensor_copy(self.util[0:1, 8:9],
                                                 self.stat[0:1, 0:1]), inc=1)
        pb.acquire(5)
        p.wait("pe", "dve", dv)
        mm = p.on("pe", lambda t_: t_.matmul(
            pb.t[5][:, 0:1], self.cst["ones_row"][0:1, :],
            self.util[0:1, 8:9], start=True, stop=True), inc=1)
        pb.wrote(5, "pe", mm)
        pb.reading(5, "act")
        ca = p.on("act", lambda a: a.activation(out=km2, in_=pb.t[5][:, 0:1],
                                                func=AF.Copy), inc=1)
        pb.read_done(5, "act", ca)

        for hf in range(2):
            for h in range(NH):
                self._l1_head(s, hf, h, BT, CKT, CKV, QH, gbuf, km2,
                              numax, wpart, recw)
            p.wait("pe", "act", p.cnt["act"])
            for qt in range(8):
                qt_g = hf * 8 + qt
                for ech in range(2):
                    b = ech
                    pb.acquire(b)
                    for hh in range(8):
                        h = ech * 8 + hh
                        p.on("pe", lambda t_, b=b, h=h, hh=hh, qt=qt: t_.matmul(
                            pb.t[b][:, hh * 64:(hh + 1) * 64],
                            self.ctxT[:, h * 1024 + qt * 128:
                                      h * 1024 + qt * 128 + 128],
                            veff[:, h * 64:(h + 1) * 64],
                            start=True, stop=True),
                            inc=(1 if hh == 7 else 0))
                    mm = p.cnt["pe"]
                    pb.wrote(b, "pe", mm)
                    pb.reading(b, "dve")
                    p.wait("dve", "act", self.last_ln_act)
                    p.dep("dve")
                    for hh in range(8):
                        h = ech * 8 + hh
                        c = p.on("dve", lambda v, b=b, h=h, hh=hh, qt=qt,
                                 ech=ech, s=s, qt_g=qt_g: v.scalar_tensor_tensor(
                            out=self.f32a[:, ech * 512 + hh * 64:
                                          ech * 512 + hh * 64 + 64],
                            in0=pb.t[b][:, hh * 64:(hh + 1) * 64],
                            scalar=recw[:, h * 8 + qt:h * 8 + qt + 1],
                            in1=self.res_tile(s, qt_g)[:, ech * 512 + hh * 64:
                                                       ech * 512 + hh * 64 + 64],
                            op0=ALU.mult, op1=ALU.add,
                            accum_out=self.stat[:, 8 + ech * 8 + hh:
                                                9 + ech * 8 + hh]), inc=1)
                    pb.read_done(b, "dve", p.cnt["dve"])
                self._ln_from_f32a(s, qt_g, nsum=16, write_fp32_out=False,
                                   stage=False)

    def _l1_head(self, s, hf, h, BT, CKT, CKV, QH, gbuf, km2, numax,
                 wpart, recw):
        p, pb = self.p, self.pb
        p.wait("act", "pe", p.cnt["pe"])
        for tc in range(2):
            b = tc % 2
            pb.acquire(b)
            mm = p.on("pe", lambda t_, b=b, tc=tc, hf=hf, h=h: t_.matmul(
                pb.t[b][:, 0:512], gbuf[:, h * 128:(h + 1) * 128],
                BT[:, hf * 1024 + tc * 512:hf * 1024 + tc * 512 + 512],
                start=True, stop=True), inc=1)
            pb.wrote(b, "pe", mm)
            pb.reading(b, "act")
            ca = p.on("act", lambda a, b=b, tc=tc: a.activation(
                out=QH[:, tc * 512:(tc + 1) * 512], in_=pb.t[b][:, 0:512],
                func=AF.Copy, scale=0.125), inc=1)
            pb.read_done(b, "act", ca)
        q2 = self.scr[:, 0:1024]
        p.dep("act")
        a_sq = p.on("act", lambda a: a.activation(out=q2, in_=QH[:, 0:1024],
                                                  func=AF.Square), inc=1)
        p.wait("pe", "act", a_sq)
        p.wait("act", "dve", p.cnt["dve"])
        for c in range(2):
            pb.acquire(5)
            mm = p.on("pe", lambda t_, c=c: t_.matmul(
                pb.t[5][0:1, 0:512], self.cst["onescol"][:, 0:1],
                q2[:, c * 512:(c + 1) * 512], start=True, stop=True), inc=1)
            pb.wrote(5, "pe", mm)
            pb.reading(5, "act")
            ca = p.on("act", lambda a, c=c: a.activation(
                out=self.f32b[0:1, c * 512:(c + 1) * 512],
                in_=pb.t[5][0:1, 0:512], func=AF.Copy), inc=1)
            pb.read_done(5, "act", ca)
        p.wait("dve", "act", p.cnt["act"])
        p.dep("dve")
        p.on("dve", lambda v: v.reduce_max(
            self.stat[0:1, 2:3], self.f32b[0:1, 0:1024], axis=AX.X), inc=1)
        p.dep("dve")
        dv = p.on("dve", lambda v: v.tensor_copy(self.util[0:1, 9:10],
                                                 self.stat[0:1, 2:3]), inc=1)
        pb.acquire(5)
        p.wait("pe", "dve", dv)
        mm = p.on("pe", lambda t_: t_.matmul(
            pb.t[5][:, 0:1], self.cst["ones_row"][0:1, :],
            self.util[0:1, 9:10], start=True, stop=True), inc=1)
        pb.wrote(5, "pe", mm)
        pb.reading(5, "act")
        ca = p.on("act", lambda a: a.activation(
            out=self.stat[:, 33:34], in_=pb.t[5][:, 0:1], func=AF.Copy),
            inc=1)
        pb.read_done(5, "act", ca)
        p.wait("dve", "act", ca)
        p.dep("dve")
        p.on("dve", lambda v: v.tensor_mul(self.stat[:, 33:34],
                                           self.stat[:, 33:34], km2))
        dvm = p.marknop("dve")
        p.wait("act", "dve", dvm)
        p.dep("act")
        ca = p.on("act", lambda a: a.activation(
            out=self.stat[:, 34:35], in_=self.stat[:, 33:34], func=AF.Sqrt),
            inc=1)
        p.wait("dve", "act", ca)
        p.dep("dve")
        p.on("dve", lambda v: v.tensor_scalar_mul(numax, self.stat[:, 34:35],
                                                  -1.0))
        p.dep("dve")
        dv_u = p.on("dve", lambda v: v.tensor_scalar_add(numax, numax, -2.0),
                    inc=1)
        p.wait("act", "dve", dv_u)
        for qt in range(8):
            p.wait("act", "dve", p.cnt["dve"])
            qt_g = hf * 8 + qt
            kend = (qt_g + 1) * 128
            nk = (kend + 511) // 512
            for c in range(nk):
                Lc = min(512, kend - c * 512)
                b = c % 2
                pb.acquire(b)
                mm = p.on("pe", lambda t_, b=b, qt=qt, c=c, Lc=Lc: t_.matmul(
                    pb.t[b][:, 0:Lc], QH[:, qt * 128:(qt + 1) * 128],
                    CKT[:, c * 512:c * 512 + Lc], start=True, stop=True),
                    inc=1)
                pb.wrote(b, "pe", mm)
                if c == nk - 1:
                    pb.reading(b, "dve")
                    dmm = p.on("dve", lambda v, b=b, Lc=Lc: v.tensor_add(
                        pb.t[b][:, Lc - 128:Lc], pb.t[b][:, Lc - 128:Lc],
                        self.cst["mask_tri"][:, :]), inc=1)
                    pb.read_done(b, "dve", dmm)
                    p.wait("act", "dve", dmm)
                pb.reading(b, "act")
                p.dep("act")
                ca = p.on("act", lambda a, b=b, c=c, Lc=Lc: a.activation(
                    out=self.pbuf[:, 0:Lc], in_=pb.t[b][:, 0:Lc], func=AF.Exp,
                    bias=numax, accum_out=wpart[:, c:c + 1]), inc=1)
                pb.read_done(b, "act", ca)
            p.wait("dve", "act", p.cnt["act"])
            p.dep("dve")
            p.on("dve", lambda v, nk=nk: v.tensor_reduce(
                out=self.stat[:, 100:101], in_=wpart[:, 0:nk], axis=AX.X,
                op=ALU.add), inc=1)
            p.dep("dve")
            p.on("dve", lambda v, h=h, qt=qt: v.reciprocal(
                recw[:, h * 8 + qt:h * 8 + qt + 1], self.stat[:, 100:101]),
                inc=1)
        av_last = {0: 0, 1: 0}
        cb = 4
        for qs in range(2):
            qs_g = hf * 2 + qs
            pb.acquire(cb)
            nkt = 4 * (qs_g + 1)
            for kt in range(nkt):
                a0 = max(0, 128 * (kt - 4 * qs_g))
                b = 2 + kt % 2
                pb.acquire(b)
                mm = p.on("pe", lambda t_, b=b, kt=kt, qs=qs, a0=a0: t_.matmul(
                    pb.t[b][:, a0:512], CKT[:, kt * 128:(kt + 1) * 128],
                    QH[:, qs * 512 + a0:(qs + 1) * 512], start=True, stop=True),
                    inc=1)
                pb.wrote(b, "pe", mm)
                half = kt % 2
                ph = self.pbuf[:, half * 512:half * 512 + 512]
                if kt >= 4 * qs_g:
                    pb.reading(b, "dve")
                    dz = p.on("dve", lambda v, b=b, a0=a0: v.tensor_add(
                        pb.t[b][:, a0:a0 + 128], pb.t[b][:, a0:a0 + 128],
                        self.cst["zmask"][:, :]), inc=1)
                    pb.read_done(b, "dve", dz)
                    p.wait("act", "dve", dz)
                p.wait("act", "pe", av_last[half])
                pb.reading(b, "act")
                p.dep("act")
                ca = p.on("act", lambda a, b=b, a0=a0, ph=ph: a.activation(
                    out=ph[:, a0:512], in_=pb.t[b][:, a0:512], func=AF.Exp,
                    bias=numax), inc=1)
                pb.read_done(b, "act", ca)
                p.wait("pe", "act", ca)
                av = p.on("pe", lambda t_, kt=kt, a0=a0, ph=ph, qs=qs,
                          nkt=nkt: t_.matmul(
                    pb.t[cb][:, a0:512], CKV[:, kt * 128:(kt + 1) * 128],
                    ph[:, a0:512], start=(kt == 0), stop=(kt == nkt - 1)),
                    inc=1)
                av_last[half] = av
            pb.wrote(cb, "pe", p.cnt["pe"])
            pb.reading(cb, "act")
            ca = p.on("act", lambda a, h=h, hf=hf, qs=qs: a.activation(
                out=self.ctxT[:, h * 1024 + qs * 512:h * 1024 + qs * 512 + 512],
                in_=pb.t[cb][:, 0:512], func=AF.Copy), inc=1)
            pb.read_done(cb, "act", ca)

    def _repl_scalar(self, src_ap, dst_col, negate=False):
        p, pb = self.p, self.pb
        pb.acquire(5)
        p.wait("pe", "dve", p.cnt["dve"])
        mm = p.on("pe", lambda t_: t_.matmul(
            pb.t[5][:, 0:1], self.cst["ones_f32"][0:1, :], src_ap,
            start=True, stop=True), inc=1)
        pb.wrote(5, "pe", mm)
        pb.reading(5, "act")
        ca = p.on("act", lambda a: a.activation(
            out=dst_col, in_=pb.t[5][:, 0:1], func=AF.Copy,
            scale=-1.0 if negate else 1.0), inc=1)
        pb.read_done(5, "act", ca)
        return ca

    def _wquant(self, wname, nrow, stash_r0, scol):
        p, pb = self.p, self.pb
        ncol = 4096 if nrow == 1024 else 1024
        rc_n, cc_n = nrow // 128, ncol // 512
        acc = self.stat[:, 40:41]
        slots = [self.f32a[:, 0:512], self.f32a[:, 512:1024]]
        p.dep("dve")
        p.on("dve", lambda v: v.memset(acc, 0.0), inc=1)
        gate = 0
        for i in range(rc_n * cc_n):
            rc, cc = divmod(i, cc_n)
            sl = slots[i % 2]
            sm = "poold" if i % 2 == 0 else "poold2"
            p.wait("pool", "act", gate)
            ld = p.dma("pool", sl,
                       self.w[wname][rc * 128:(rc + 1) * 128,
                                     cc * 512:(cc + 1) * 512], sem=sm)
            p.wait("act", sm, ld)
            p.wait("act", "dve", p.cnt["dve"])
            p.dep("act")
            ca = p.on("act", lambda a, sl=sl: a.activation(
                out=self.scr[:, 0:512], in_=sl, func=AF.Abs,
                accum_out=self.stat[:, 41:42]), inc=1)
            gate = ca
            p.wait("dve", "act", ca)
            p.dep("dve")
            p.on("dve", lambda v: v.tensor_add(acc, acc, self.stat[:, 41:42]),
                 inc=1)
        dv = p.cnt["dve"]
        pb.acquire(5)
        p.wait("pe", "dve", dv)
        mm = p.on("pe", lambda t_: t_.matmul(
            pb.t[5][0:1, 0:1], acc, self.cst["onescol_f32"][:, 0:1],
            start=True, stop=True), inc=1)
        pb.wrote(5, "pe", mm)
        pb.reading(5, "act")
        ca = p.on("act", lambda a, n=float(nrow * ncol): a.activation(
            out=self.stat[0:1, 43:44], in_=pb.t[5][0:1, 0:1], func=AF.Copy,
            scale=1.0 / n), inc=1)
        pb.read_done(5, "act", ca)
        p.wait("dve", "act", ca)
        p.dep("dve")
        p.on("dve", lambda v: v.tensor_scalar_max(
            self.stat[0:1, 43:44], self.stat[0:1, 43:44], 1e-5))
        p.dep("dve")
        p.on("dve", lambda v: v.reciprocal(self.stat[0:1, 44:45],
                                           self.stat[0:1, 43:44]), inc=1)
        self._repl_scalar(bass.AP(self.stat, 44, [[256, 1], [1, 1]]), scol)
        sca = p.cnt["act"]
        p.wait("dve", "act", sca)
        gate_d = 0
        stash_cnt = p.cnt["siod"]
        for i in range(rc_n * cc_n):
            rc, cc = divmod(i, cc_n)
            sl = slots[i % 2]
            sm = "poold" if i % 2 == 0 else "poold2"
            p.wait("pool", "dve", gate_d)
            ld = p.dma("pool", sl,
                       self.w[wname][rc * 128:(rc + 1) * 128,
                                     cc * 512:(cc + 1) * 512], sem=sm)
            p.wait("dve", sm, ld)
            p.dep("dve")
            dvw = p.on("dve", lambda v, sl=sl, scol=scol: v.tensor_scalar_mul(
                self.f32b[:, 0:512], sl, scol), inc=1)
            p.wait("act", "dve", dvw)
            p.dep("act")
            p.on("act", lambda a: a.activation(
                out=self.f32b[:, 512:1024], in_=self.f32b[:, 0:512],
                func=AF.Abs))
            p.dep("act")
            p.on("act", lambda a: a.activation(
                out=self.f32b[:, 512:1024], in_=self.f32b[:, 512:1024],
                func=AF.Sign, bias=self.cst["biases"][:, 258:259]))
            p.dep("act")
            aw = p.on("act", lambda a: a.activation(
                out=self.f32b[:, 0:512], in_=self.f32b[:, 0:512],
                func=AF.Sign), inc=1)
            p.wait("dve", "act", aw)
            p.dep("dve")
            p.on("dve", lambda v: v.tensor_scalar_mul(
                self.f32b[:, 512:1024], self.f32b[:, 512:1024], 0.5))
            p.dep("dve")
            p.on("dve", lambda v: v.tensor_scalar_add(
                self.f32b[:, 512:1024], self.f32b[:, 512:1024], 0.5))
            p.wait("dve", "siod", stash_cnt)
            p.dep("dve")
            c = p.on("dve", lambda v: v.tensor_mul(
                self.scr[:, 512:1024], self.f32b[:, 0:512],
                self.f32b[:, 512:1024]), inc=1)
            gate_d = c
            p.wait("sio", "dve", c)
            p.wait("sio", "siod", p.cnt["siod"])
            stash_cnt = p.dma("sio", self.stage_w[stash_r0 + rc * 128:
                                                  stash_r0 + (rc + 1) * 128,
                                                  cc * 512:(cc + 1) * 512],
                              self.scr[:, 512:1024])

    def _round_cols(self, src_ap, dst_ap, n, i8off=0):
        p = self.p
        i8 = self.i8t[:, i8off:i8off + n]
        p.dep("dve")
        p.on("dve", lambda v: v.tensor_copy(i8, src_ap))
        p.dep("dve")
        p.on("dve", lambda v: v.tensor_copy(dst_ap, i8))
        p.dep("dve")
        dv = p.on("dve", lambda v: v.tensor_sub(self.f32b[:, 0:n], src_ap,
                                                dst_ap), inc=1)
        p.wait("act", "dve", dv)
        p.dep("act")
        p.on("act", lambda a: a.activation(
            out=self.f32a[:, 0:n], in_=self.f32b[:, 0:n], func=AF.Sign,
            bias=self.cst["biases"][:, 258:259]))
        p.dep("act")
        aw = p.on("act", lambda a: a.activation(
            out=self.f32b[:, 0:n], in_=self.f32b[:, 0:n], func=AF.Sign,
            bias=self.cst["biases"][:, 259:260]), inc=1)
        p.wait("dve", "act", aw)
        p.dep("dve")
        p.on("dve", lambda v: v.tensor_add(self.f32b[:, 0:n],
                                           self.f32b[:, 0:n],
                                           self.f32a[:, 0:n]))
        p.dep("dve")
        c = p.on("dve", lambda v: v.scalar_tensor_tensor(
            out=dst_ap, in0=self.f32b[:, 0:n], scalar=0.5, in1=dst_ap,
            op0=ALU.mult, op1=ALU.add), inc=1)
        return c

    def l1_ffn(self):
        p, pb = self.p, self.pb
        sw1 = self.stat[:, 45:46]
        sw2 = self.stat[:, 46:47]
        self._wquant("w1T", 1024, 0, sw1)
        self._wquant("w2T", 4096, 1024, sw2)
        if self.debug:
            p.probe(self, "w1q", self.stage_w[0:128, 0:512], [128, 512], BF16)
            p.probe(self, "w2q", self.stage_w[1024:1152, 0:512], [128, 512],
                    BF16)
            p.probe(self, "scal", self.stat[:, 45:47], [128, 2], F32)
        p.wait("pool", "dve", p.cnt["dve"])
        ld = p.dma("pool", self.f32a[0:1, 0:1024], self.w["n1w"][0:1, :])
        p.dma("pool", self.stat[:, 4:5], self.w["asc"][:, :])
        ld2 = p.dma("pool", self.stat[:, 5:6], self.w["abi"][:, :])
        p.wait("pe", "poold", ld2)
        for c2 in range(2):
            pb.acquire(5)
            mm = p.on("pe", lambda t_, c2=c2: t_.matmul(
                pb.t[5][:, 0:512], self.cst["ones_f32"][0:1, :],
                self.f32a[0:1, c2 * 512:(c2 + 1) * 512],
                start=True, stop=True), inc=1)
            pb.wrote(5, "pe", mm)
            pb.reading(5, "act")
            ca = p.on("act", lambda a, c2=c2: a.activation(
                out=self.crows[:, c2 * 512:(c2 + 1) * 512],
                in_=pb.t[5][:, 0:512], func=AF.Copy), inc=1)
            pb.read_done(5, "act", ca)
        p.barrier(("pe", "act", "dve", "sio", "pool"))
        deq = self.stat[:, 160:192]
        sxw = self.stat[:, 48:49]
        rstd1 = self.stat[:, 49:50]
        n1rdy = p.cnt["act"]
        p.wait("act", "siod", p.cnt["siod"])
        stq_gate = 0
        for tt in range(32):
            s_, t_ = divmod(tt, NT)
            p.wait("act", "dve", p.cnt["dve"])
            p.dep("act")
            ca = p.on("act", lambda a, s_=s_, t_=t_: a.activation(
                out=self.scr[:, 1024:2048], in_=self.res_tile(s_, t_),
                func=AF.Square, accum_out=self.stat[:, 50:51]), inc=1)
            p.dep("act")
            ca = p.on("act", lambda a: a.activation(
                out=self.stat[:, 51:52], in_=self.stat[:, 50:51], func=AF.Sqrt,
                bias=self.cst["biases"][:, 257:258], scale=1.0 / DM), inc=1)
            p.wait("dve", "act", ca)
            p.wait("dve", "siod", stq_gate)
            p.dep("dve")
            p.on("dve", lambda v: v.reciprocal(rstd1, self.stat[:, 51:52]))
            p.dep("dve")
            p.on("dve", lambda v, s_=s_, t_=t_: v.tensor_mul(
                self.f32a[:, 0:1024], self.res_tile(s_, t_),
                self.crows[:, 0:1024]))
            p.dep("dve")
            p.on("dve", lambda v: v.tensor_reduce(
                out=self.stat[:, 52:53], in_=self.f32a[:, 0:1024], axis=AX.X,
                op=ALU.max, apply_absolute_value=True))
            p.dep("dve")
            p.on("dve", lambda v: v.tensor_mul(self.stat[:, 53:54],
                                               self.stat[:, 52:53], rstd1))
            p.dep("dve")
            p.on("dve", lambda v: v.tensor_scalar_max(
                self.stat[:, 53:54], self.stat[:, 53:54], 1e-5))
            p.dep("dve")
            p.on("dve", lambda v: v.reciprocal(self.stat[:, 54:55],
                                               self.stat[:, 53:54]))
            p.dep("dve")
            p.on("dve", lambda v: v.tensor_scalar_mul(
                sxw, self.stat[:, 54:55], 127.0))
            p.dep("dve")
            p.on("dve", lambda v: v.tensor_mul(self.stat[:, 55:56], sxw,
                                               rstd1))
            p.dep("dve")
            p.on("dve", lambda v: v.tensor_scalar_mul(
                self.f32b[:, 0:1024], self.f32a[:, 0:1024],
                bass.AP(self.stat, 55, [[256, 128], [1, 1]])))
            c = self._round_cols(self.f32b[:, 0:1024], self.scr[:, 0:1024],
                                 1024)
            p.dep("dve")
            p.on("dve", lambda v: v.tensor_mul(self.stat[:, 56:57], sxw, sw1))
            p.dep("dve")
            p.on("dve", lambda v, tt=tt: v.reciprocal(deq[:, tt:tt + 1],
                                                      self.stat[:, 56:57]),
                 inc=1)
            p.wait("sio", "dve", c)
            p.wait("sio", "siod", p.cnt["siod"])
            r0 = s_ * SEQ + t_ * 128
            stq_gate = p.dma("sio", self.stage_a[r0:r0 + 128, :],
                             self.scr[:, 0:1024])
        if self.debug:
            p.probe(self, "xq", self.stage_a[0:128, 0:1024], [128, 1024], BF16)
            p.probe(self, "xq1", self.stage_a[128:256, 0:1024], [128, 1024],
                    BF16)
            p.probe(self, "xq16", self.stage_a[2048:2176, 0:1024], [128, 1024],
                    BF16)
            p.probe(self, "deq1", deq[:, 0:32], [128, 32], F32)
        p.wait("pool", "pe", p.cnt["pe"])
        p.wait("pool", "dve", p.cnt["dve"])
        ld = p.dma("pool", self.f32a[0:1, 0:1024], None) if False else 0
        for c4 in range(4):
            ldn = p.dma("pool", self.f32a[0:1, 0:1024],
                        self.w["n2w"][0:1, c4 * 1024:(c4 + 1) * 1024])
            p.wait("pe", "poold", ldn)
            for c2 in range(2):
                pb.acquire(5)
                mm = p.on("pe", lambda t_, c2=c2: t_.matmul(
                    pb.t[5][:, 0:512], self.cst["ones_f32"][0:1, :],
                    self.f32a[0:1, c2 * 512:(c2 + 1) * 512],
                    start=True, stop=True), inc=1)
                pb.wrote(5, "pe", mm)
                pb.reading(5, "act")
                ca = p.on("act", lambda a, c4=c4, c2=c2: a.activation(
                    out=self.ctxT[:, c4 * 1024 + c2 * 512:
                                  c4 * 1024 + c2 * 512 + 512],
                    in_=pb.t[5][:, 0:512], func=AF.Copy), inc=1)
                pb.read_done(5, "act", ca)
                p.wait("pool", "pe", p.cnt["pe"])
        p.barrier(("pe", "act", "dve", "sio", "pool"))
        rstd2c = self.stat[:, 64:96]
        hsq = self.stat[:, 192:256]
        w1s = [self.xT[:, 8192 + r * 4096:8192 + (r + 1) * 4096]
               for r in range(2)]
        xqts = self.xT[:, 0:8192]
        arh_gate = 0
        for ts in range(4):
            p.wait("sio", "pe", p.cnt["pe"])
            p.wait("sio", "siod", p.cnt["siod"])
            for dc in range(8):
                p.dma("sio", xqts[:, dc * 1024:(dc + 1) * 1024],
                      self.stage_a[ts * 1024:(ts + 1) * 1024,
                                   dc * 128:(dc + 1) * 128], transpose=True)
            xq_ld = p.cnt["siod"]
            p.wait("pe", "siod", xq_ld)
            p.wait("sio", "siod", xq_ld)
            w1gate = 0
            for jc in range(8):
                sl = w1s[jc % 2]
                p.wait("sio", "pe", w1gate)
                for dc in range(8):
                    ld = p.dma("sio", sl[:, dc * 512:(dc + 1) * 512],
                               self.stage_w[dc * 128:(dc + 1) * 128,
                                            jc * 512:(jc + 1) * 512])
                p.wait("pe", "siod", p.cnt["siod"])
                for tt in range(8):
                    tt_g = ts * 8 + tt
                    b = tt % 2
                    pb.acquire(b)
                    for dc in range(8):
                        p.on("pe", lambda t_e, b=b, dc=dc, tt=tt, sl=sl:
                             t_e.matmul(
                            pb.t[b][:, 0:512],
                            xqts[:, dc * 1024 + tt * 128:
                                 dc * 1024 + tt * 128 + 128],
                            sl[:, dc * 512:(dc + 1) * 512],
                            start=(dc == 0), stop=(dc == 7)),
                            inc=(1 if dc == 7 else 0))
                    mm = p.cnt["pe"]
                    if tt == 7:
                        w1gate = mm
                    pb.wrote(b, "pe", mm)
                    rh = self.pbuf[:, 0:512]
                    arh = self.pbuf[:, 512:1024]
                    pb.reading(b, "act")
                    p.wait("act", "dve", p.cnt["dve"])
                    p.dep("act")
                    ca = p.on("act", lambda a, b=b, tt_g=tt_g, rh=rh:
                              a.activation(
                        out=rh, in_=pb.t[b][:, 0:512], func=AF.Relu,
                        scale=deq[:, tt_g:tt_g + 1]), inc=1)
                    pb.read_done(b, "act", ca)
                    p.wait("dve", "act", ca)
                    p.wait("dve", "siod2", arh_gate)
                    p.dep("dve")
                    dv = p.on("dve", lambda v, rh=rh, arh=arh:
                              v.scalar_tensor_tensor(
                        out=arh, in0=rh,
                        scalar=bass.AP(self.stat, 4, [[256, 128], [1, 1]]),
                        in1=rh, op0=ALU.mult, op1=ALU.mult), inc=1)
                    p.wait("act", "dve", dv)
                    p.dep("act")
                    ca = p.on("act", lambda a, arh=arh, tt=tt, jc=jc:
                              a.activation(
                        out=self.scr[:, 0:512], in_=arh, func=AF.Square,
                        bias=self.stat[:, 5:6],
                        accum_out=hsq[:, tt * 8 + jc:tt * 8 + jc + 1]),
                        inc=1)
                    p.wait("sio", "dve", dv)
                    p.wait("sio", "siod2", p.cnt["siod2"])
                    r0 = (ts * 8 + tt) * 128
                    arh_gate = p.dma(
                        "sio", bass.AP(self.stage_b, r0 * 4096 + jc * 512,
                                       [[4096, 128], [1, 512]]), arh,
                        sem="siod2")
            p.wait("dve", "act", p.cnt["act"])
            for tt in range(8):
                tt_g = ts * 8 + tt
                p.dep("dve")
                p.on("dve", lambda v, tt=tt: v.tensor_reduce(
                    out=self.stat[:, 57:58], in_=hsq[:, tt * 8:tt * 8 + 8],
                    axis=AX.X, op=ALU.add), inc=1)
                dvv = p.marknop("dve")
                p.wait("act", "dve", dvv)
                p.dep("act")
                ca = p.on("act", lambda a: a.activation(
                    out=self.stat[:, 58:59], in_=self.stat[:, 57:58],
                    func=AF.Sqrt, bias=self.cst["biases"][:, 257:258],
                    scale=1.0 / 4096.0), inc=1)
                p.wait("dve", "act", ca)
                p.dep("dve")
                p.on("dve", lambda v, tt_g=tt_g: v.reciprocal(
                    rstd2c[:, tt_g:tt_g + 1], self.stat[:, 58:59]), inc=1)
        if self.debug:
            p.probe(self, "arh2", bass.AP(self.stage_b, 0, [[4096, 128],
                                                            [1, 4096]]),
                    [128, 4096], BF16)
            p.probe(self, "rstd2", rstd2c[:, 0:32], [128, 32], F32)
        p.barrier(("pe", "act", "dve", "sio", "pool"))
        h2w0 = self.ctxT[:, 4096:6144]
        h2w1 = self.ctxT[:, 6144:8192]
        arhld = self.ctxT[:, 12288:14336]
        hqo = self.ctxT[:, 8192:12288]
        arh_rd = 0
        st_gate = 0
        for tt in range(32):
            r0 = tt * 128
            for half in range(2):
                p.wait("sio", "dve", arh_rd)
                ld = p.dma("sio", arhld,
                           bass.AP(self.stage_b, r0 * 4096 + half * 2048,
                                   [[4096, 128], [1, 2048]]))
                p.wait("dve", "siod", ld)
                h2w = h2w0 if half == 0 else h2w1
                p.dep("dve")
                arh_rd = p.on("dve", lambda v, h2w=h2w, half=half:
                              v.scalar_tensor_tensor(
                    out=h2w, in0=arhld,
                    scalar=bass.AP(self.stat, 5, [[256, 128], [1, 1]]),
                    in1=self.ctxT[:, half * 2048:(half + 1) * 2048],
                    op0=ALU.add, op1=ALU.mult), inc=1)
                p.dep("dve")
                p.on("dve", lambda v, h2w=h2w, half=half: v.tensor_reduce(
                    out=self.stat[:, 59 + half:60 + half], in_=h2w, axis=AX.X,
                    op=ALU.max, apply_absolute_value=True), inc=1)
            p.dep("dve")
            p.on("dve", lambda v: v.tensor_max(
                self.stat[:, 59:60], self.stat[:, 59:60], self.stat[:, 60:61]))
            p.dep("dve")
            p.on("dve", lambda v, tt=tt: v.tensor_mul(
                self.stat[:, 61:62], self.stat[:, 59:60], rstd2c[:, tt:tt + 1]))
            p.dep("dve")
            p.on("dve", lambda v: v.tensor_scalar_max(
                self.stat[:, 61:62], self.stat[:, 61:62], 1e-5))
            p.dep("dve")
            p.on("dve", lambda v: v.reciprocal(self.stat[:, 62:63],
                                               self.stat[:, 61:62]))
            p.dep("dve")
            p.on("dve", lambda v: v.tensor_scalar_mul(
                self.stat[:, 62:63], self.stat[:, 62:63], 127.0))
            p.dep("dve")
            p.on("dve", lambda v, tt=tt: v.tensor_mul(
                self.stat[:, 63:64], self.stat[:, 62:63], rstd2c[:, tt:tt + 1]))
            p.dep("dve")
            p.on("dve", lambda v: v.tensor_mul(self.stat[:, 57:58],
                                               self.stat[:, 62:63], sw2))
            p.dep("dve")
            p.on("dve", lambda v, tt=tt: v.reciprocal(deq[:, tt:tt + 1],
                                                      self.stat[:, 57:58]),
                 inc=1)
            for half in range(2):
                h2w = h2w0 if half == 0 else h2w1
                p.wait("dve", "siod2", st_gate)
                for qq in range(2):
                    p.dep("dve")
                    p.on("dve", lambda v, h2w=h2w, qq=qq: v.tensor_scalar_mul(
                        self.f32b[:, 0:1024],
                        h2w[:, qq * 1024:(qq + 1) * 1024],
                        bass.AP(self.stat, 63, [[256, 128], [1, 1]])))
                    c = self._round_cols(
                        self.f32b[:, 0:1024],
                        hqo[:, half * 2048 + qq * 1024:
                            half * 2048 + qq * 1024 + 1024], 1024)
                p.wait("sio", "dve", c)
                p.wait("sio", "siod2", p.cnt["siod2"])
                st_gate = p.dma("sio",
                                self.stage_c[r0:r0 + 128,
                                             half * 2048:(half + 1) * 2048],
                                hqo[:, half * 2048:(half + 1) * 2048],
                                sem="siod2")
        p.barrier(("pe", "act", "dve", "sio", "pool"))
        if self.debug:
            p.probe(self, "hq", self.stage_c[0:128, 0:4096], [128, 4096], BF16)
            p.probe(self, "deq2", deq[:, 0:32], [128, 32], F32)
        xab = self.ctxT[:, 0:4096]
        _ = None
        w2s = [self.wbuf[:, i * 512:(i + 1) * 512] for i in range(4)]
        p.wait("sio", "siod2", p.cnt["siod2"])
        for ts in range(8):
            p.wait("sio", "pe", p.cnt["pe"])
            p.wait("sio", "dve", p.cnt["dve"])
            for jc in range(32):
                p.dma("sio", self.xT[:, jc * 512:(jc + 1) * 512],
                      self.stage_c[ts * 512:(ts + 1) * 512,
                                   jc * 128:(jc + 1) * 128], transpose=True)
            hq_ld = p.cnt["siod"]
            p.wait("pe", "siod", hq_ld)
            p.wait("sio", "siod", hq_ld)
            w2gate = p.cnt["pe"]
            for ech in range(2):
                for tt in range(4):
                    pb.acquire(tt)
                for jg in range(8):
                    p.wait("sio", "pe", w2gate)
                    for j4 in range(4):
                        jc = jg * 4 + j4
                        p.dma("sio", w2s[j4],
                              self.stage_w[1024 + jc * 128:
                                           1024 + (jc + 1) * 128,
                                           ech * 512:(ech + 1) * 512])
                    p.wait("pe", "siod", p.cnt["siod"])
                    for tt in range(4):
                        for j4 in range(4):
                            jc = jg * 4 + j4
                            p.on("pe", lambda t_e, tt=tt, j4=j4, jc=jc:
                                 t_e.matmul(
                                pb.t[tt][:, 0:512],
                                self.xT[:, jc * 512 + tt * 128:
                                        jc * 512 + tt * 128 + 128],
                                w2s[j4][:, :],
                                start=(jc == 0), stop=(jc == 31)),
                                inc=(1 if j4 == 3 else 0))
                    w2gate = p.cnt["pe"]
                for tt in range(4):
                    tt_g = ts * 4 + tt
                    s_, t_ = divmod(tt_g, NT)
                    pb.wrote(tt, "pe", w2gate)
                    pb.reading(tt, "dve")
                    p.wait("dve", "act", self.last_ln_act)
                    p.dep("dve")
                    c = p.on("dve", lambda v, tt=tt, tt_g=tt_g, s_=s_, t_=t_,
                             ech=ech: v.scalar_tensor_tensor(
                        out=xab[:, tt * 1024 + ech * 512:
                                tt * 1024 + ech * 512 + 512],
                        in0=pb.t[tt][:, 0:512],
                        scalar=deq[:, tt_g:tt_g + 1],
                        in1=self.res_tile(s_, t_)[:, ech * 512:
                                                  (ech + 1) * 512],
                        op0=ALU.mult, op1=ALU.add,
                        accum_out=self.stat[:, 8 + 2 * tt + ech:
                                            9 + 2 * tt + ech]), inc=1)
                    pb.read_done(tt, "dve", c)
            for tt in range(4):
                tt_g = ts * 4 + tt
                s_, t_ = divmod(tt_g, NT)
                self._ln_from_f32a(s_, t_, nsum=2, write_fp32_out=True,
                                   stage=False,
                                   src=xab[:, tt * 1024:(tt + 1) * 1024],
                                   stat_base=8 + 2 * tt)

    def build(self):
        p = self.p
        self.load_consts()
        self.load_input()
        p.barrier()
        if "l0a" in self.blocks and self.l0_stage >= 1:
            self.l0_attn()
            p.barrier()
        if "l0f" in self.blocks:
            self.l0_ffn()
            p.barrier()
        if "l1a" in self.blocks:
            self.l1_attn()
            p.barrier()
        if "l1f" in self.blocks:
            self.l1_ffn()
            p.barrier()
        if self.l0_stage < 4 or "l1f" not in self.blocks:
            self.copy_out()
            p.barrier()
        p.wait("sio", "siod", p.cnt["siod"])
        p.wait("pool", "poold", p.cnt["poold"])
        p.finalize(self.ctx)
        return self.nc


def build_inmaps(inputs):
    consts, _ = make_consts()
    wp = prep_weights(inputs)
    x = np.ascontiguousarray(np.asarray(inputs["L0_attn_x"], dtype=np.float32))
    in_maps = []
    for core in range(NCORES):
        m = {}
        xs = x[core * NS:(core + 1) * NS].reshape(NS * SEQ, DM)
        m["x_in"] = np.ascontiguousarray(xs)
        for k, v in wp.items():
            m[k] = v
        for k, v in consts.items():
            m["c_" + k] = np.ascontiguousarray(v.astype(np.float32))
        in_maps.append(m)
    return in_maps


def kernel(**inputs):
    mk = MK()
    nc = mk.build()
    in_maps = build_inmaps(inputs)
    res = run_bass_kernel_spmd(nc, in_maps, core_ids=list(range(NCORES)))
    out = np.concatenate([r["y_out"].reshape(NS, SEQ, DM)
                          for r in res.results], axis=0)
    return out.astype(np.float32)
